# version 05
# baseline (speedup 1.0000x reference)
import numpy as np
from contextlib import ExitStack

import concourse.bass as bass
import concourse.mybir as mybir
from concourse.bass_utils import run_bass_kernel_spmd

F32 = mybir.dt.float32
BF16 = mybir.dt.bfloat16
F16 = mybir.dt.float16
I32 = mybir.dt.int32
AF = mybir.ActivationFunctionType
ALU = mybir.AluOpType

D = 1024
INNER = 2048
HID = 3072
SEQ = 4096
BPC = 2
KD = D // 128
KI = INNER // 128
KH = HID // 128
NT = 512
NCH = SEQ // NT
NIT = BPC * NCH
EPS = 1e-5
MAGIC = 0x5F3759DF
R127 = 1.0 / 127.0


def _rsqrt_seed(v, out_f32, in_f32):
    ob = out_f32.bitcast(I32)
    v.tensor_scalar(ob, in_f32.bitcast(I32), 1, None, ALU.arith_shift_right)
    v.tensor_scalar(ob, ob, -1, MAGIC, ALU.mult, ALU.add)


def _rsqrt_newton(eng, s, x, tmp, n=2):
    for _ in range(n):
        eng.tensor_tensor(tmp, s, s, ALU.mult)
        eng.tensor_tensor(tmp, tmp, x, ALU.mult)
        eng.tensor_scalar(tmp, tmp, -0.5, 1.5, ALU.mult, ALU.add)
        eng.tensor_tensor(s, s, tmp, ALU.mult)


def build_program(expose=(), phases=(1, 2, 3, 4, 5, 6)):
    nc = bass.Bass("TRN2", target_bir_lowering=False)

    def dram(name, shape, dt):
        if name in expose:
            return nc.dram_tensor(name, shape, dt, kind="ExternalOutput")
        return nc.dram_tensor(name, shape, dt)

    x_h = nc.dram_tensor("x", [BPC * D * SEQ], F32, kind="ExternalInput")
    w_in_h = nc.dram_tensor("w_in", [D * 2 * INNER], F32, kind="ExternalInput")
    w_delta_h = nc.dram_tensor("w_delta", [INNER * INNER], F32, kind="ExternalInput")
    w_out_h = nc.dram_tensor("w_out", [INNER * D], F32, kind="ExternalInput")
    wg_h = nc.dram_tensor("wgT", [D * HID], I32, kind="ExternalInput")
    wu_h = nc.dram_tensor("wuT", [D * HID], I32, kind="ExternalInput")
    wd_h = nc.dram_tensor("wdT", [HID * D], I32, kind="ExternalInput")
    convd_h = nc.dram_tensor("convd", [3 * KI * 128 * 128], F32, kind="ExternalInput")
    gam_h = nc.dram_tensor("gam", [128 * KD], F32, kind="ExternalInput")
    bet_h = nc.dram_tensor("bet", [128 * KD], F32, kind="ExternalInput")
    b_in_h = nc.dram_tensor("b_in", [128 * 32], F32, kind="ExternalInput")
    conv_b_h = nc.dram_tensor("conv_b", [128 * KI], F32, kind="ExternalInput")
    b_delta_h = nc.dram_tensor("b_delta", [128 * KI], F32, kind="ExternalInput")
    b_out_h = nc.dram_tensor("b_out", [D], F32, kind="ExternalInput")

    out_h = nc.dram_tensor("out", [BPC * D * SEQ], F32, kind="ExternalOutput")

    s_vpre = dram("s_vpre", [BPC * INNER * SEQ], BF16)
    s_sg = dram("s_sg", [BPC * INNER * SEQ], BF16)
    s_y = dram("s_y", [BPC * D * SEQ], BF16)
    s_a = dram("s_a", [BPC * INNER * SEQ], F16)
    s_b = dram("s_b", [BPC * INNER * SEQ], BF16)
    s_fwd = dram("s_fwd", [BPC * INNER * SEQ], BF16)
    s_at = dram("s_at", [BPC * D * SEQ], BF16)
    s_x2 = dram("s_x2", [BPC * D * SEQ], F32)
    s_u = dram("s_u", [BPC * D * SEQ], BF16)
    s_h = dram("s_h", [BPC * HID * SEQ], BF16)

    AP = bass.AP

    def x_ap(s, c, n=NT):
        return AP(x_h, s * D * SEQ + c * NT, [[SEQ, 128], [128 * SEQ, KD], [1, n]])

    def scr_ap(t, s, c, nch, n=NT, off=0):
        return AP(t, s * nch * 128 * SEQ + c * NT + off,
                  [[SEQ, 128], [128 * SEQ, nch], [1, n]])

    es = ExitStack()
    with es:
        def sb(name, shape, dt):
            return es.enter_context(nc.sbuf_tensor(name, shape, dt))

        def sem(name):
            return es.enter_context(nc.semaphore(name))

        ones128 = sb("ones128", [128, 128], BF16)
        onesrow = sb("onesrow", [1, NT], BF16)
        b_out_row = sb("b_out_row", [1, D], BF16)
        gam = sb("gam_s", [128, KD], F32)
        bet = sb("bet_s", [128, KD], F32)
        b_in = sb("b_in_s", [128, 32], F32)
        conv_b = sb("conv_b_s", [128, KI], F32)
        b_delta = sb("b_delta_s", [128, KI], F32)
        negbd = sb("negbd", [128, KI], F32)
        fcarry = sb("fcarry", [128, KI], F32)
        bcarry = sb("bcarry", [128, KI], F32)

        ps = [es.enter_context(nc.psum_tensor(f"ps{i}", [128, NT], F32)) for i in range(8)]

        sp_vpre, sp_sg, sp_y = sem("sp_vpre"), sem("sp_sg"), sem("sp_y")
        sp_a, sp_b, sp_fwd = sem("sp_a"), sem("sp_b"), sem("sp_fwd")
        sp_at, sp_x2, sp_u, sp_h = sem("sp_at"), sem("sp_x2"), sem("sp_u"), sem("sp_h")
        s_od = sem("s_od")
        s_w1, s_w2, s_w3, s_w4, s_w5 = [sem(f"s_w{i}") for i in range(1, 6)]
        sx, s_cast, s_stats, s_vr = sem("sx"), sem("s_cast"), sem("s_stats"), sem("s_vr")
        s_pipe, s_h2, s_mm, s_ev = sem("s_pipe"), sem("s_h2"), sem("s_mm"), sem("s_ev")
        s_vp, s_vpm, s_cmm, s_vt = sem("s_vp"), sem("s_vpm"), sem("s_cmm"), sem("s_vt")
        s_dmm, s_d16, s_vb, s_scan = sem("s_dmm"), sem("s_d16"), sem("s_vb"), sem("s_scan")
        s_l3, s_as, s_bs, s_fu = sem("s_l3"), sem("s_as"), sem("s_bs"), sem("s_fu")
        s_l3f = sem("s_l3f")
        s_om, s_oe = sem("s_om"), sem("s_oe")
        s_l3b, s_c3, s_st3, s_vr3, s_p3 = sem("s_l3b"), sem("s_c3"), sem("s_st3"), sem("s_vr3"), sem("s_p3")
        s_x2s, s_c4, s_st4, s_vr4, s_p4, s_u_ = sem("s_x2s"), sem("s_c4"), sem("s_st4"), sem("s_vr4"), sem("s_p4"), sem("s_u_")
        s_l4, s_gm, s_um, s_si, s_h4 = sem("s_l4"), sem("s_gm"), sem("s_um"), sem("s_si"), sem("s_h4")
        s_l5, s_dm, s_fe, s_c5 = sem("s_l5"), sem("s_dm"), sem("s_fe"), sem("s_c5")
        s_st5, s_vr5, s_p5, s_out5 = sem("s_st5"), sem("s_vr5"), sem("s_p5"), sem("s_out5")
        s_lx2 = sem("s_lx2")
        s_nb = sem("s_nb")
        s_se, s_se3, s_se4, s_se5 = sem("s_se"), sem("s_se3"), sem("s_se4"), sem("s_se5")

        NW1 = 8 + 6

        with ExitStack() as ph:
          if 1 in phases:
              psb = lambda n, sh, dt: ph.enter_context(nc.sbuf_tensor(n, sh, dt))
              w_in_sb = psb("w_in_sb", [128, KD * 4096], BF16)
              xt = [psb(f"xt{i}", [128, KD * NT], F32) for i in range(2)]
              xb = psb("xb", [128, KD * NT], BF16)
              xsq = psb("xsq", [128, KD * NT], BF16)
              t0f = psb("t0f", [128, KD * NT], F32)
              yb = psb("yb", [128, KD * NT], BF16)
              h2 = psb("h2", [128, KD * NT], BF16)
              vpre = psb("vpre", [128, KI * NT], BF16)
              sg = psb("sg", [128, KI * NT], BF16)
              M1b = [psb(f"M1b{i}", [128, NT], F32) for i in range(2)]
              S1b = [psb(f"S1b{i}", [128, NT], F32) for i in range(2)]
              Ab = [psb(f"Ab{i}", [128, NT], F32) for i in range(2)]
              vet = [psb(f"vet{i}", [128, NT], F32) for i in range(2)]
              tp1 = psb("tp1", [128, NT], F32)
              gp1 = psb("gp1", [128, NT], F32)
              gp2 = psb("gp2", [128, NT], F32)

              with nc.Block("ph1") as blk:
                  @blk.gpsimd
                  def _(g):
                      g.memset(ones128[:, :], 1.0)
                      g.memset(onesrow[:, :], 1.0)
                      for k in range(KD):
                          g.dma_start(out=w_in_sb[:, k * 4096:(k + 1) * 4096],
                                      in_=AP(w_in_h, k * 128 * 4096, [[4096, 128], [1, 4096]])
                                      ).then_inc(s_w1, 16)
                      g.dma_start(out=gam[:, :], in_=AP(gam_h, 0, [[KD, 128], [1, KD]])).then_inc(s_w1, 16)
                      g.dma_start(out=bet[:, :], in_=AP(bet_h, 0, [[KD, 128], [1, KD]])).then_inc(s_w1, 16)
                      g.dma_start(out=b_in[:, :], in_=AP(b_in_h, 0, [[32, 128], [1, 32]])).then_inc(s_w1, 16)
                      g.dma_start(out=conv_b[:, :], in_=AP(conv_b_h, 0, [[KI, 128], [1, KI]])).then_inc(s_w1, 16)
                      g.dma_start(out=b_delta[:, :], in_=AP(b_delta_h, 0, [[KI, 128], [1, KI]])).then_inc(s_w1, 16)
                      g.dma_start(out=b_out_row[:, :], in_=AP(b_out_h, 0, [[D, 1], [1, D]])).then_inc(s_w1, 16)
                      for i in range(NIT + 1):
                          if i >= 1:
                              j = i - 1
                              s, c = j // NCH, j % NCH
                              g.wait_ge(s_ev, 32 * i)
                              g.dma_start(out=scr_ap(s_vpre, s, c, KI),
                                          in_=AP(vpre, 0, [[KI * NT, 128], [NT, KI], [1, NT]])
                                          ).then_inc(sp_vpre, 16)
                              g.dma_start(out=scr_ap(s_sg, s, c, KI),
                                          in_=AP(sg, 0, [[KI * NT, 128], [NT, KI], [1, NT]])
                                          ).then_inc(sp_sg, 16)
                              g.wait_ge(s_h2, i)
                              g.dma_start(out=scr_ap(s_y, s, c, KD),
                                          in_=AP(yb, 0, [[KD * NT, 128], [NT, KD], [1, NT]])
                                          ).then_inc(sp_y, 16)

                  @blk.sync
                  def _(sy):
                      for i in range(NIT):
                          s, c = i // NCH, i % NCH
                          if i >= 2:
                              sy.wait_ge(s_cast, i - 1)
                              sy.wait_ge(s_h2, i - 1)
                          sy.dma_start(out=AP(xt[i % 2], 0, [[KD * NT, 128], [NT, KD], [1, NT]]),
                                       in_=x_ap(s, c)).then_inc(sx, 16)

                  @blk.scalar
                  def _(s_):
                      for i in range(NIT + 1):
                          if i < NIT:
                              s_.wait_ge(sx, 16 * (i + 1))
                              s_.wait_ge(s_stats, i)
                              s_.activation(xb[:, :], xt[i % 2][:, :], AF.Copy)
                              s_.activation(xsq[:, :], xt[i % 2][:, :], AF.Square).then_inc(s_cast, 1)
                              b = i % 2
                              s_.wait_ge(s_stats, i + 1)
                              s_.wait_ge(s_h2, max(0, i - 1))
                              s_.mul(M1b[b][:, :], ps[4][:, :], 1.0 / D)
                              s_.mul(vet[b][:, :], ps[5][:, :], 1.0 / D).then_inc(s_se, 1)
                          if i >= 1:
                              j = i - 1
                              for m in range(32):
                                  s_.wait_ge(s_mm, 32 * j + m + 1)
                                  if m == 0:
                                      s_.wait_ge(sp_vpre, 16 * j)
                                  if m == 16:
                                      s_.wait_ge(sp_sg, 16 * j)
                                  bank = ps[m % 4]
                                  if m < 16:
                                      s_.activation(vpre[:, m * NT:(m + 1) * NT], bank[:, :],
                                                    AF.Identity, bias=b_in[:, m:m + 1]
                                                    ).then_inc(s_ev, 1)
                                  else:
                                      s_.activation(sg[:, (m - 16) * NT:(m - 15) * NT], bank[:, :],
                                                    AF.Sigmoid, bias=b_in[:, m:m + 1]
                                                    ).then_inc(s_ev, 1)

                  @blk.tensor
                  def _(t):
                      t.wait_ge(s_w1, 16 * NW1)
                      for i in range(NIT + 1):
                          if i < NIT:
                              t.wait_ge(s_cast, i + 1)
                              t.wait_ge(s_se, i)
                              for k in range(KD):
                                  t.matmul(ps[4][:, :], ones128[:, :], xb[:, k * NT:(k + 1) * NT],
                                           start=(k == 0), stop=(k == KD - 1))
                              for k in range(KD):
                                  mm = t.matmul(ps[5][:, :], ones128[:, :], xsq[:, k * NT:(k + 1) * NT],
                                                start=(k == 0), stop=(k == KD - 1))
                              mm.then_inc(s_stats, 1)
                          if i >= 1:
                              j = i - 1
                              t.wait_ge(s_h2, i)
                              for m in range(32):
                                  gidx = 32 * j + m
                                  if gidx >= 4:
                                      t.wait_ge(s_ev, gidx - 3)
                                  bank = ps[m % 4]
                                  for k in range(KD):
                                      mm = t.matmul(bank[:, :],
                                                    w_in_sb[:, k * 4096 + m * 128: k * 4096 + (m + 1) * 128],
                                                    h2[:, k * NT:(k + 1) * NT],
                                                    start=(k == 0), stop=(k == KD - 1))
                                  mm.then_inc(s_mm, 1)

                  @blk.vector
                  def _(v):
                      for i in range(NIT + 1):
                          if i < NIT:
                              b = i % 2
                              v.wait_ge(s_se, i + 1)
                              v.tensor_tensor(tp1[:, :], M1b[b][:, :], M1b[b][:, :], ALU.mult)
                              v.tensor_tensor(vet[b][:, :], vet[b][:, :], tp1[:, :], ALU.subtract)
                              v.tensor_scalar(vet[b][:, :], vet[b][:, :], 1.0, EPS, ALU.mult, ALU.add)
                              _rsqrt_seed(v, S1b[b][:, :], vet[b][:, :])
                              _rsqrt_newton(v, S1b[b][:, :], vet[b][:, :], gp1[:, :])
                              v.tensor_scalar(gp2[:, :], vet[b][:, :], 1.0 + EPS, -EPS, ALU.mult, ALU.add)
                              v.tensor_tensor(gp1[:, :], S1b[b][:, :], S1b[b][:, :], ALU.mult)
                              v.tensor_tensor(gp1[:, :], gp1[:, :], gp2[:, :], ALU.mult)
                              v.tensor_scalar(gp1[:, :], gp1[:, :], -0.5, 1.5, ALU.mult, ALU.add)
                              v.tensor_tensor(Ab[b][:, :], S1b[b][:, :], gp1[:, :], ALU.mult).then_inc(s_vr, 1)
                          if i >= 1:
                              j = i - 1
                              b = j % 2
                              v.wait_ge(sp_y, 16 * j)
                              v.wait_ge(s_mm, 32 * j)
                              bc = lambda tile: AP(tile, 0, [[NT, 128], [0, KD], [1, NT]])
                              v.tensor_tensor(t0f[:, :], xt[b][:, :], bc(M1b[b]), ALU.subtract)
                              v.tensor_tensor(yb[:, :], t0f[:, :], bc(S1b[b]), ALU.mult)
                              v.tensor_tensor(t0f[:, :], t0f[:, :], bc(Ab[b]), ALU.mult)
                              for k in range(KD):
                                  ts_i = v.tensor_scalar(h2[:, k * NT:(k + 1) * NT], t0f[:, k * NT:(k + 1) * NT],
                                                  gam[:, k:k + 1], bet[:, k:k + 1], ALU.mult, ALU.add)
                              ts_i.then_inc(s_h2, 1)

        PH2LVL = 8
        with ExitStack() as ph:
          if 2 in phases:
              psb = lambda n, sh, dt: ph.enter_context(nc.sbuf_tensor(n, sh, dt))
              w_delta_sb = psb("w_delta_sb", [128, KI * 2048], BF16)
              diag_sb = psb("diag_sb", [128, 3 * KI * 128], BF16)
              VPW = NT + 2
              vp = [psb(f"vp{i}", [128, KI * VPW], BF16) for i in range(2)]
              vt = psb("vt", [128, KI * NT], BF16)
              dm1t = psb("dm1t", [128, KI * NT], F16)
              bt = psb("bt", [128, KI * NT], BF16)
              scano = psb("scano", [128, KI * NT], BF16)
              a_s = [psb(f"a_s{i}", [128, NT], F32) for i in range(2)]
              NW2 = KI + 3 * KI

              with nc.Block("ph2") as blk:
                  @blk.gpsimd
                  def _(g):
                      for k in range(KI):
                          g.dma_start(out=w_delta_sb[:, k * 2048:(k + 1) * 2048],
                                      in_=AP(w_delta_h, k * 128 * 2048, [[2048, 128], [1, 2048]])
                                      ).then_inc(s_w2, 16)
                      for kk in range(3 * KI):
                          g.dma_start(out=diag_sb[:, kk * 128:(kk + 1) * 128],
                                      in_=AP(convd_h, kk * 128 * 128, [[128, 128], [1, 128]])
                                      ).then_inc(s_w2, 16)
                      for i in range(NIT + 1):
                          if i < NIT and i % NCH == 0:
                              if PH2LVL >= 2:
                                  g.wait_ge(s_cmm, max(0, 16 * (i - 1)))
                              g.memset(AP(vp[i % 2], 0, [[KI * VPW, 128], [VPW, KI], [1, 2]]), 0.0).then_inc(s_vpm, 1)
                          if PH2LVL >= 8 and i >= 1:
                              j = i - 1
                              s, c = j // NCH, j % NCH
                              g.wait_ge(s_d16, 16 * i)
                              g.dma_start(out=scr_ap(s_a, s, c, KI),
                                          in_=AP(dm1t, 0, [[KI * NT, 128], [NT, KI], [1, NT]])
                                          ).then_inc(sp_a, 16)
                              g.wait_ge(s_vb, i)
                              g.dma_start(out=scr_ap(s_b, s, c, KI),
                                          in_=AP(bt, 0, [[KI * NT, 128], [NT, KI], [1, NT]])
                                          ).then_inc(sp_b, 16)
                              g.wait_ge(s_scan, 16 * i)
                              g.dma_start(out=scr_ap(s_fwd, s, c, KI),
                                          in_=AP(scano, 0, [[KI * NT, 128], [NT, KI], [1, NT]])
                                          ).then_inc(sp_fwd, 16)

                  @blk.sync
                  def _(sy):
                      for i in range(NIT):
                          s, c = i // NCH, i % NCH
                          sy.wait_ge(sp_vpre, 16 * (i + 1))
                          if PH2LVL >= 2 and i >= 2:
                              sy.wait_ge(s_cmm, 16 * (i - 1))
                          if c == 0:
                              sy.dma_start(out=AP(vp[i % 2], 2, [[KI * VPW, 128], [VPW, KI], [1, NT]]),
                                           in_=scr_ap(s_vpre, s, c, KI)).then_inc(s_vp, 16)
                          else:
                              sy.dma_start(out=AP(vp[i % 2], 0, [[KI * VPW, 128], [VPW, KI], [1, VPW]]),
                                           in_=scr_ap(s_vpre, s, c, KI, n=VPW, off=-2)).then_inc(s_vp, 16)

                  @blk.tensor
                  def _(t):
                      t.wait_ge(s_w2, 16 * NW2)
                      t.wait_ge(s_nb, 1)
                      for i in range(NIT + 1):
                          if PH2LVL >= 4 and i >= 1:
                              j = i - 1
                              t.wait_ge(s_vt, 16 * i)
                              for m in range(KI):
                                  gidx = KI * j + m
                                  if PH2LVL >= 5 and gidx >= 4:
                                      t.wait_ge(s_d16, gidx - 3)
                                  bank = ps[4 + m % 4]
                                  for k in range(KI):
                                      mm = t.matmul(bank[:, :],
                                                    w_delta_sb[:, k * 2048 + m * 128: k * 2048 + (m + 1) * 128],
                                                    vt[:, k * NT:(k + 1) * NT],
                                                    start=(k == 0), stop=(k == KI - 1))
                                  mm.then_inc(s_dmm, 1)
                          if PH2LVL >= 2 and i < NIT:
                              t.wait_ge(s_vp, 16 * (i + 1))
                              t.wait_ge(s_vpm, i // NCH + 1)
                              for cc in range(KI):
                                  gidx = KI * i + cc
                                  if PH2LVL >= 3 and gidx >= 4:
                                      t.wait_ge(s_vt, gidx - 3)
                                  bank = ps[cc % 4]
                                  for k3 in range(3):
                                      mm = t.matmul(bank[:, :],
                                                    diag_sb[:, (k3 * KI + cc) * 128:(k3 * KI + cc + 1) * 128],
                                                    vp[i % 2][:, cc * VPW + k3: cc * VPW + k3 + NT],
                                                    start=(k3 == 0), stop=(k3 == 2))
                                  mm.then_inc(s_cmm, 1)

                  @blk.scalar
                  def _(s_):
                      s_.wait_ge(s_w1, 16 * NW1)
                      s_.wait_ge(s_nb, 1)
                      for i in range(NIT + 1):
                          if PH2LVL >= 5 and i >= 1:
                              j = i - 1
                              for m in range(KI):
                                  s_.wait_ge(s_dmm, 16 * j + m + 1)
                                  if m == 0:
                                      if PH2LVL >= 8:
                                          s_.wait_ge(sp_a, 16 * j)
                                      if PH2LVL >= 7:
                                          s_.wait_ge(s_scan, 16 * j)
                                  s_.activation(dm1t[:, m * NT:(m + 1) * NT], ps[4 + m % 4][:, :],
                                                AF.Sigmoid, bias=negbd[:, m:m + 1]
                                                ).then_inc(s_d16, 1)
                          if PH2LVL >= 3 and i < NIT:
                              for cc in range(KI):
                                  s_.wait_ge(s_cmm, 16 * i + cc + 1)
                                  if cc == 0:
                                      if PH2LVL >= 4:
                                          s_.wait_ge(s_dmm, 16 * i)
                                      if PH2LVL >= 6:
                                          s_.wait_ge(s_vb, i)
                                  s_.activation(vt[:, cc * NT:(cc + 1) * NT], ps[cc % 4][:, :],
                                                AF.Silu, bias=conv_b[:, cc:cc + 1]).then_inc(s_vt, 1)

                  @blk.vector
                  def _(v):
                      v.wait_ge(s_w1, 16 * NW1)
                      v.tensor_scalar(negbd[:, :], b_delta[:, :], -1.0, None, ALU.mult)
                      v.wait_ge(s_w2, 16 * KI)
                      v.tensor_scalar(w_delta_sb[:, :], w_delta_sb[:, :], -1.0, None, ALU.mult).then_inc(s_nb, 1)
                      for i in range(1, NIT + 1):
                          if PH2LVL < 6:
                              break
                          j = i - 1
                          s, c = j // NCH, j % NCH
                          v.wait_ge(s_d16, 16 * i)
                          if PH2LVL >= 8:
                              v.wait_ge(sp_b, 16 * j)
                          v.tensor_tensor(bt[:, :], dm1t[:, :], vt[:, :], ALU.mult).then_inc(s_vb, 1)
                          if PH2LVL < 7:
                              continue
                          if PH2LVL >= 8:
                              v.wait_ge(sp_fwd, 16 * j)
                          for cc in range(KI):
                              v.tensor_scalar(a_s[cc % 2][:, :], dm1t[:, cc * NT:(cc + 1) * NT],
                                              -1.0, 1.0, ALU.mult, ALU.add)
                              init = 0.0 if c == 0 else fcarry[:, cc:cc + 1]
                              v.tensor_tensor_scan(scano[:, cc * NT:(cc + 1) * NT],
                                                   a_s[cc % 2][:, :], bt[:, cc * NT:(cc + 1) * NT],
                                                   init, ALU.mult, ALU.add).then_inc(s_scan, 1)
                          v.tensor_copy(fcarry[:, :],
                                        AP(scano, NT - 1, [[KI * NT, 128], [NT, KI]]))

        with ExitStack() as ph:
          if 3 in phases:
              psb = lambda n, sh, dt: ph.enter_context(nc.sbuf_tensor(n, sh, dt))
              w_out_sb = psb("w_out_sb", [128, KI * 1024], BF16)
              a_l = [psb(f"a_l{i}", [128, KI * NT], F16) for i in range(2)]
              b_l = [psb(f"b_l{i}", [128, KI * NT], BF16) for i in range(2)]
              f_l1 = psb("f_l1", [128, KI * NT], BF16)
              g_l1 = psb("g_l1", [128, KI * NT], BF16)
              y_l1 = psb("y_l1", [128, KD * NT], BF16)
              f_l = [f_l1, f_l1]
              g_l = [g_l1, g_l1]
              y_l = [y_l1, y_l1]
              bwdt = psb("bwdt", [128, KI * NT], BF16)
              fu1 = psb("fu1", [128, KI * NT], BF16)
              att = psb("att", [128, KD * NT], BF16)
              a_s3 = [psb(f"a_s3{i}", [128, NT], F32) for i in range(2)]
              NI3 = BPC * NCH

              def p2sc(p):
                  return p // NCH, NCH - 1 - (p % NCH)

              with nc.Block("ph3a") as blk:
                  @blk.gpsimd
                  def _(g):
                      for k in range(KI):
                          g.dma_start(out=w_out_sb[:, k * 1024:(k + 1) * 1024],
                                      in_=AP(w_out_h, k * 128 * 1024, [[1024, 128], [1, 1024]])
                                      ).then_inc(s_w3, 16)
                      for p in range(1, NI3 + 1):
                          j = p - 1
                          s, c = p2sc(j)
                          g.wait_ge(s_oe, 8 * p)
                          g.dma_start(out=scr_ap(s_at, s, c, KD),
                                      in_=AP(att, 0, [[KD * NT, 128], [NT, KD], [1, NT]])
                                      ).then_inc(sp_at, 16)

                  @blk.sync
                  def _(sy):
                      sy.wait_ge(sp_a, 16 * NIT)
                      sy.wait_ge(sp_b, 16 * NIT)
                      sy.wait_ge(sp_fwd, 16 * NIT)
                      sy.wait_ge(sp_sg, 16 * NIT)
                      sy.wait_ge(sp_y, 16 * NIT)
                      for p in range(NI3):
                          s, c = p2sc(p)
                          if p >= 2:
                              sy.wait_ge(s_bs, 16 * (p - 1))
                          bb = p % 2
                          sy.dma_start(out=AP(a_l[bb], 0, [[KI * NT, 128], [NT, KI], [1, NT]]),
                                       in_=scr_ap(s_a, s, c, KI)).then_inc(s_l3, 16)
                          sy.dma_start(out=AP(b_l[bb], 0, [[KI * NT, 128], [NT, KI], [1, NT]]),
                                       in_=scr_ap(s_b, s, c, KI)).then_inc(s_l3, 16)
                          if p >= 1:
                              sy.wait_ge(s_fu, p)
                              sy.wait_ge(s_oe, 8 * p)
                          sy.dma_start(out=AP(f_l1, 0, [[KI * NT, 128], [NT, KI], [1, NT]]),
                                       in_=scr_ap(s_fwd, s, c, KI)).then_inc(s_l3f, 16)
                          sy.dma_start(out=AP(g_l1, 0, [[KI * NT, 128], [NT, KI], [1, NT]]),
                                       in_=scr_ap(s_sg, s, c, KI)).then_inc(s_l3f, 16)
                          sy.dma_start(out=AP(y_l1, 0, [[KD * NT, 128], [NT, KD], [1, NT]]),
                                       in_=scr_ap(s_y, s, c, KD)).then_inc(s_l3f, 16)

                  @blk.tensor
                  def _(t):
                      t.wait_ge(s_w3, 16 * KI)
                      t.wait_ge(s_w1, 16 * NW1)
                      for p in range(1, NI3 + 1):
                          j = p - 1
                          t.wait_ge(s_fu, p)
                          for m in range(8):
                              gidx = 8 * j + m
                              if gidx >= 4:
                                  t.wait_ge(s_oe, gidx - 3)
                              bank = ps[m % 4]
                              for k in range(KI):
                                  t.matmul(bank[:, :],
                                           w_out_sb[:, k * 1024 + m * 128: k * 1024 + (m + 1) * 128],
                                           fu1[:, k * NT:(k + 1) * NT],
                                           start=(k == 0), stop=False)
                              mm = t.matmul(bank[:, :], b_out_row[0:1, m * 128:(m + 1) * 128],
                                            onesrow[0:1, 0:NT], start=False, stop=True)
                              mm.then_inc(s_om, 1)

                  @blk.vector
                  def _(v):
                      nas = 0
                      for p in range(1, NI3 + 1):
                          j = p - 1
                          s, c = p2sc(j)
                          bb = j % 2
                          v.wait_ge(s_l3, 32 * p)
                          for cc in range(KI):
                              v.tensor_scalar(a_s3[cc % 2][:, :], a_l[bb][:, cc * NT:(cc + 1) * NT],
                                              -1.0, 1.0, ALU.mult, ALU.add).then_inc(s_as, 1)
                              nas += 1
                              v.wait_ge(s_as, nas)
                              init = 0.0 if c == NCH - 1 else bcarry[:, cc:cc + 1]
                              v.tensor_tensor_scan(
                                  AP(bwdt, cc * NT + NT - 1, [[KI * NT, 128], [-1, NT]]),
                                  AP(a_s3[cc % 2], NT - 1, [[NT, 128], [-1, NT]]),
                                  AP(b_l[bb], cc * NT + NT - 1, [[KI * NT, 128], [-1, NT]]),
                                  init, ALU.mult, ALU.add).then_inc(s_bs, 1)
                          v.wait_ge(s_bs, 16 * p)
                          v.tensor_copy(bcarry[:, :], AP(bwdt, 0, [[KI * NT, 128], [NT, KI]]))
                          v.wait_ge(s_om, 8 * j)
                          v.wait_ge(s_l3f, 48 * p)
                          v.tensor_tensor(fu1[:, :], f_l[bb][:, :], bwdt[:, :], ALU.add)
                          v.tensor_tensor(fu1[:, :], fu1[:, :], g_l[bb][:, :], ALU.mult).then_inc(s_fu, 1)
                          for m in range(8):
                              v.wait_ge(s_om, 8 * j + m + 1)
                              if m == 0:
                                  v.wait_ge(sp_at, 16 * j)
                              v.tensor_tensor(att[:, m * NT:(m + 1) * NT], ps[m % 4][:, :],
                                              y_l[bb][:, m * NT:(m + 1) * NT], ALU.add
                                              ).then_inc(s_oe, 1)

        with ExitStack() as ph:
          if 4 in phases:
              psb = lambda n, sh, dt: ph.enter_context(nc.sbuf_tensor(n, sh, dt))
              xt3 = [psb(f"xt3{i}", [128, KD * NT], F32) for i in range(2)]
              at3 = [psb(f"at3{i}", [128, KD * NT], BF16) for i in range(2)]
              sqat = psb("sqat", [128, KD * NT], BF16)
              x2t = [psb(f"x2t{i}", [128, KD * NT], F32) for i in range(2)]
              x2b = psb("x2b", [128, KD * NT], BF16)
              x2sq = psb("x2sq", [128, KD * NT], BF16)
              ub = psb("ub", [128, KD * NT], BF16)
              q1 = psb("q1", [128, KD * NT], F32)
              q2 = psb("q2", [128, KD * NT], F32)
              M3b = [psb(f"M3b{i}", [128, NT], F32) for i in range(2)]
              S3b = [psb(f"S3b{i}", [128, NT], F32) for i in range(2)]
              ve3 = [psb(f"ve3{i}", [128, NT], F32) for i in range(2)]
              M4b = [psb(f"M4b{i}", [128, NT], F32) for i in range(2)]
              S4b = [psb(f"S4b{i}", [128, NT], F32) for i in range(2)]
              ve4 = [psb(f"ve4{i}", [128, NT], F32) for i in range(2)]
              tq1 = psb("tq1", [128, NT], F32)
              gq1 = psb("gq1", [128, NT], F32)
              gq2 = psb("gq2", [128, NT], F32)

              with nc.Block("ph3b") as blk:
                  @blk.sync
                  def _(sy):
                      sy.wait_ge(sp_at, 16 * NIT)
                      for g in range(NIT):
                          s, c = g // NCH, g % NCH
                          if g >= 2:
                              sy.wait_ge(s_x2s, g - 1)
                          bb = g % 2
                          sy.dma_start(out=AP(xt3[bb], 0, [[KD * NT, 128], [NT, KD], [1, NT]]),
                                       in_=x_ap(s, c)).then_inc(s_l3b, 16)
                          sy.dma_start(out=AP(at3[bb], 0, [[KD * NT, 128], [NT, KD], [1, NT]]),
                                       in_=scr_ap(s_at, s, c, KD)).then_inc(s_l3b, 16)

                  @blk.scalar
                  def _(s_):
                      for g in range(NIT + 1):
                          if g < NIT:
                              s_.wait_ge(s_l3b, 32 * (g + 1))
                              s_.wait_ge(s_st3, g)
                              s_.activation(sqat[:, :], at3[g % 2][:, :], AF.Square).then_inc(s_c3, 1)
                              b = g % 2
                              s_.wait_ge(s_st3, g + 1)
                              s_.wait_ge(s_x2s, max(0, g - 1))
                              s_.mul(M3b[b][:, :], ps[4][:, :], 1.0 / D)
                              s_.mul(ve3[b][:, :], ps[5][:, :], 1.0 / D).then_inc(s_se3, 1)
                          if g >= 1:
                              s_.wait_ge(s_x2s, g)
                              s_.wait_ge(s_st4, g - 1)
                              s_.activation(x2b[:, :], x2t[(g - 1) % 2][:, :], AF.Copy)
                              s_.activation(x2sq[:, :], x2t[(g - 1) % 2][:, :], AF.Square).then_inc(s_c4, 1)
                              j = g - 1
                              b = j % 2
                              s_.wait_ge(s_st4, g)
                              s_.wait_ge(s_u_, max(0, j - 1))
                              s_.mul(M4b[b][:, :], ps[6][:, :], 1.0 / D)
                              s_.mul(ve4[b][:, :], ps[7][:, :], 1.0 / D).then_inc(s_se4, 1)

                  @blk.tensor
                  def _(t):
                      for g in range(NIT + 1):
                          if g < NIT:
                              t.wait_ge(s_c3, g + 1)
                              t.wait_ge(s_se3, g)
                              for k in range(KD):
                                  t.matmul(ps[4][:, :], ones128[:, :], at3[g % 2][:, k * NT:(k + 1) * NT],
                                           start=(k == 0), stop=(k == KD - 1))
                              for k in range(KD):
                                  mm = t.matmul(ps[5][:, :], ones128[:, :], sqat[:, k * NT:(k + 1) * NT],
                                                start=(k == 0), stop=(k == KD - 1))
                              mm.then_inc(s_st3, 1)
                          if g >= 1:
                              t.wait_ge(s_c4, g)
                              t.wait_ge(s_se4, g - 1)
                              for k in range(KD):
                                  t.matmul(ps[6][:, :], ones128[:, :], x2b[:, k * NT:(k + 1) * NT],
                                           start=(k == 0), stop=(k == KD - 1))
                              for k in range(KD):
                                  mm = t.matmul(ps[7][:, :], ones128[:, :], x2sq[:, k * NT:(k + 1) * NT],
                                                start=(k == 0), stop=(k == KD - 1))
                              mm.then_inc(s_st4, 1)

                  @blk.vector
                  def _(v):
                      bc = lambda tile: AP(tile, 0, [[NT, 128], [0, KD], [1, NT]])
                      for g in range(NIT + 2):
                          if g < NIT:
                              b = g % 2
                              v.wait_ge(s_se3, g + 1)
                              v.tensor_tensor(tq1[:, :], M3b[b][:, :], M3b[b][:, :], ALU.mult)
                              v.tensor_tensor(ve3[b][:, :], ve3[b][:, :], tq1[:, :], ALU.subtract)
                              v.tensor_scalar(ve3[b][:, :], ve3[b][:, :], 1.0, EPS, ALU.mult, ALU.add)
                              _rsqrt_seed(v, S3b[b][:, :], ve3[b][:, :])
                              _rsqrt_newton(v, S3b[b][:, :], ve3[b][:, :], gq1[:, :])
                          if g >= 2:
                              j = g - 2
                              b = j % 2
                              v.wait_ge(sp_u, 16 * max(0, j - 1))
                              v.tensor_tensor(q1[:, :], x2t[b][:, :], bc(M4b[b]), ALU.subtract)
                              v.tensor_tensor(ub[:, :], q1[:, :], bc(S4b[b]), ALU.mult).then_inc(s_u_, 1)
                          if g >= 1 and g - 1 < NIT:
                              j = g - 1
                              b = j % 2
                              v.wait_ge(s_c4, max(0, g - 2))
                              v.wait_ge(sp_x2, 16 * max(0, j - 1))
                              v.tensor_tensor(q1[:, :], at3[b][:, :], bc(M3b[b]), ALU.subtract)
                              v.tensor_tensor(q2[:, :], q1[:, :], bc(S3b[b]), ALU.mult)
                              v.tensor_tensor(x2t[b][:, :], q2[:, :], xt3[b][:, :], ALU.add).then_inc(s_x2s, 1)
                              v.wait_ge(s_se4, g)
                              v.tensor_tensor(tq1[:, :], M4b[b][:, :], M4b[b][:, :], ALU.mult)
                              v.tensor_tensor(ve4[b][:, :], ve4[b][:, :], tq1[:, :], ALU.subtract)
                              v.tensor_scalar(ve4[b][:, :], ve4[b][:, :], 1.0, EPS, ALU.mult, ALU.add)
                              _rsqrt_seed(v, S4b[b][:, :], ve4[b][:, :])
                              _rsqrt_newton(v, S4b[b][:, :], ve4[b][:, :], gq2[:, :])

                  @blk.gpsimd
                  def _(g_):
                      for g in range(NIT + 2):
                          if g >= 1 and g - 1 < NIT:
                              j = g - 1
                              b = j % 2
                              s, c = j // NCH, j % NCH
                              g_.wait_ge(s_x2s, j + 1)
                              g_.dma_start(out=scr_ap(s_x2, s, c, KD),
                                           in_=AP(x2t[b], 0, [[KD * NT, 128], [NT, KD], [1, NT]])
                                           ).then_inc(sp_x2, 16)
                          if g >= 2:
                              j = g - 2
                              s, c = j // NCH, j % NCH
                              g_.wait_ge(s_u_, j + 1)
                              g_.dma_start(out=scr_ap(s_u, s, c, KD),
                                           in_=AP(ub, 0, [[KD * NT, 128], [NT, KD], [1, NT]])
                                           ).then_inc(sp_u, 16)

        with ExitStack() as ph:
          if 5 in phases:
              psb = lambda n, sh, dt: ph.enter_context(nc.sbuf_tensor(n, sh, dt))
              wg_sb = psb("wg_sb", [128, KD * HID], BF16)
              wu_sb = psb("wu_sb", [128, KD * HID], BF16)
              u_l = [psb(f"u_l{i}", [128, KD * NT], BF16) for i in range(2)]
              gsb = [psb(f"gsb{i}", [128, NT], BF16) for i in range(2)]
              ht = psb("ht", [128, KH * NT], BF16)

              with nc.Block("ph4") as blk:
                  @blk.gpsimd
                  def _(g):
                      for k in range(KD):
                          g.dma_start(out=wg_sb[:, k * HID:(k + 1) * HID],
                                      in_=AP(wg_h, k * 128 * HID, [[HID, 128], [1, HID]])
                                      ).then_inc(s_w4, 16)
                          g.dma_start(out=wu_sb[:, k * HID:(k + 1) * HID],
                                      in_=AP(wu_h, k * 128 * HID, [[HID, 128], [1, HID]])
                                      ).then_inc(s_w4, 16)
                      for i in range(1, NIT + 1):
                          j = i - 1
                          s, c = j // NCH, j % NCH
                          g.wait_ge(s_h4, KH * i)
                          g.dma_start(out=scr_ap(s_h, s, c, KH),
                                      in_=AP(ht, 0, [[KH * NT, 128], [NT, KH], [1, NT]])
                                      ).then_inc(sp_h, 16)

                  @blk.sync
                  def _(sy):
                      sy.wait_ge(sp_u, 16 * NIT)
                      for i in range(NIT):
                          s, c = i // NCH, i % NCH
                          if i >= 2:
                              sy.wait_ge(s_um, KH * (i - 1))
                              sy.wait_ge(s_gm, KH * (i - 1))
                          sy.dma_start(out=AP(u_l[i % 2], 0, [[KD * NT, 128], [NT, KD], [1, NT]]),
                                       in_=scr_ap(s_u, s, c, KD)).then_inc(s_l4, 16)

                  @blk.tensor
                  def _(t):
                      t.wait_ge(s_w4, 16 * 2 * KD)
                      for i in range(NIT):
                          t.wait_ge(s_l4, 16 * (i + 1))
                          for m in range(KH):
                              gidx = KH * i + m
                              if gidx >= 2:
                                  t.wait_ge(s_si, gidx - 1)
                              bank = ps[m % 2]
                              for k in range(KD):
                                  mm = t.matmul(bank[:, :],
                                                wg_sb[:, k * HID + m * 128: k * HID + (m + 1) * 128],
                                                u_l[i % 2][:, k * NT:(k + 1) * NT],
                                                start=(k == 0), stop=(k == KD - 1))
                              mm.then_inc(s_gm, 1)
                              if gidx >= 2:
                                  t.wait_ge(s_h4, gidx - 1)
                              bank = ps[2 + m % 2]
                              for k in range(KD):
                                  mm = t.matmul(bank[:, :],
                                                wu_sb[:, k * HID + m * 128: k * HID + (m + 1) * 128],
                                                u_l[i % 2][:, k * NT:(k + 1) * NT],
                                                start=(k == 0), stop=(k == KD - 1))
                              mm.then_inc(s_um, 1)

                  @blk.scalar
                  def _(s_):
                      for i in range(NIT):
                          for m in range(KH):
                              gidx = KH * i + m
                              s_.wait_ge(s_gm, gidx + 1)
                              if gidx >= 2:
                                  s_.wait_ge(s_h4, gidx - 1)
                              s_.activation(gsb[m % 2][:, :], ps[m % 2][:, :],
                                            AF.Silu, scale=R127).then_inc(s_si, 1)

                  @blk.vector
                  def _(v):
                      for i in range(NIT):
                          for m in range(KH):
                              gidx = KH * i + m
                              v.wait_ge(s_si, gidx + 1)
                              v.wait_ge(s_um, gidx + 1)
                              if m == 0:
                                  v.wait_ge(sp_h, 16 * i)
                              v.tensor_tensor(ht[:, m * NT:(m + 1) * NT], gsb[m % 2][:, :],
                                              ps[2 + m % 2][:, :], ALU.mult).then_inc(s_h4, 1)

        with ExitStack() as ph:
          if 6 in phases:
              psb = lambda n, sh, dt: ph.enter_context(nc.sbuf_tensor(n, sh, dt))
              wd_sb = psb("wd_sb", [128, KH * 1024], BF16)
              h_l = [psb(f"h_l{i}", [128, KH * NT], BF16) for i in range(2)]
              x2_l1 = psb("x2_l1", [128, KD * NT], F32)
              ff = [psb(f"ff{i}", [128, KD * NT], F32) for i in range(2)]
              ffb = psb("ffb", [128, KD * NT], BF16)
              ffsq = psb("ffsq", [128, KD * NT], BF16)
              outt1 = psb("outt1", [128, KD * NT], F32)
              outt = [outt1, outt1]
              M5b = [psb(f"M5b{i}", [128, NT], F32) for i in range(2)]
              S5b = [psb(f"S5b{i}", [128, NT], F32) for i in range(2)]
              ve5 = [psb(f"ve5{i}", [128, NT], F32) for i in range(2)]
              tr1 = psb("tr1", [128, NT], F32)
              gr1 = psb("gr1", [128, NT], F32)

              with nc.Block("ph5") as blk:
                  @blk.gpsimd
                  def _(g):
                      for k in range(KH):
                          g.dma_start(out=wd_sb[:, k * 1024:(k + 1) * 1024],
                                      in_=AP(wd_h, k * 128 * 1024, [[1024, 128], [1, 1024]])
                                      ).then_inc(s_w5, 16)

                  @blk.sync
                  def _(sy):
                      sy.wait_ge(sp_h, 16 * NIT)
                      sy.wait_ge(sp_x2, 16 * NIT)
                      for g5 in range(NIT + 2):
                          if g5 < NIT:
                              s, c = g5 // NCH, g5 % NCH
                              if g5 >= 2:
                                  sy.wait_ge(s_dm, 8 * (g5 - 1))
                              bb = g5 % 2
                              sy.dma_start(out=AP(h_l[bb], 0, [[KH * NT, 128], [NT, KH], [1, NT]]),
                                           in_=scr_ap(s_h, s, c, KH)).then_inc(s_l5, 16)
                          if g5 >= 2:
                              j = g5 - 2
                              s, c = j // NCH, j % NCH
                              sy.wait_ge(s_out5, j + 1)
                              sy.dma_start(out=AP(out_h, s * D * SEQ + c * NT,
                                                  [[SEQ, 128], [128 * SEQ, KD], [1, NT]]),
                                           in_=AP(outt[j % 2], 0, [[KD * NT, 128], [NT, KD], [1, NT]])
                                           ).then_inc(s_od, 16)
                          if g5 < NIT:
                              s, c = g5 // NCH, g5 % NCH
                              sy.wait_ge(s_out5, g5)
                              sy.dma_start(out=AP(x2_l1, 0, [[KD * NT, 128], [NT, KD], [1, NT]]),
                                           in_=scr_ap(s_x2, s, c, KD)).then_inc(s_lx2, 16)
                      sy.wait_ge(s_od, 16 * NIT)

                  @blk.tensor
                  def _(t):
                      t.wait_ge(s_w5, 16 * KH)
                      for g5 in range(NIT + 1):
                          if g5 < NIT:
                              t.wait_ge(s_l5, 16 * (g5 + 1))
                              for m in range(8):
                                  gidx = 8 * g5 + m
                                  if gidx >= 4:
                                      t.wait_ge(s_fe, gidx - 3)
                                  bank = ps[m % 4]
                                  for k in range(KH):
                                      mm = t.matmul(bank[:, :],
                                                    wd_sb[:, k * 1024 + m * 128: k * 1024 + (m + 1) * 128],
                                                    h_l[g5 % 2][:, k * NT:(k + 1) * NT],
                                                    start=(k == 0), stop=(k == KH - 1))
                                  mm.then_inc(s_dm, 1)
                          if g5 >= 1:
                              j = g5 - 1
                              t.wait_ge(s_c5, g5)
                              t.wait_ge(s_se5, max(0, j))
                              for k in range(KD):
                                  t.matmul(ps[4][:, :], ones128[:, :], ffb[:, k * NT:(k + 1) * NT],
                                           start=(k == 0), stop=(k == KD - 1))
                              for k in range(KD):
                                  mm = t.matmul(ps[5][:, :], ones128[:, :], ffsq[:, k * NT:(k + 1) * NT],
                                                start=(k == 0), stop=(k == KD - 1))
                              mm.then_inc(s_st5, 1)

                  @blk.scalar
                  def _(s_):
                      for g5 in range(NIT + 2):
                          if 1 <= g5 <= NIT:
                              j = g5 - 1
                              for m in range(8):
                                  s_.wait_ge(s_dm, 8 * j + m + 1)
                                  if m == 0:
                                      s_.wait_ge(s_out5, max(0, j - 1))
                                  s_.activation(ff[j % 2][:, m * NT:(m + 1) * NT], ps[m % 4][:, :],
                                                AF.Copy).then_inc(s_fe, 1)
                              s_.wait_ge(s_st5, j)
                              s_.activation(ffb[:, :], ff[j % 2][:, :], AF.Copy)
                              s_.activation(ffsq[:, :], ff[j % 2][:, :], AF.Square).then_inc(s_c5, 1)
                          if True:
                              if g5 >= 2:
                                  jj = g5 - 2
                                  b = jj % 2
                                  s_.wait_ge(s_st5, jj + 1)
                                  s_.wait_ge(s_out5, max(0, jj - 1))
                                  s_.mul(M5b[b][:, :], ps[4][:, :], 1.0 / D)
                                  s_.mul(ve5[b][:, :], ps[5][:, :], 1.0 / D).then_inc(s_se5, 1)

                  @blk.vector
                  def _(v):
                      bc = lambda tile: AP(tile, 0, [[NT, 128], [0, KD], [1, NT]])
                      for g5 in range(NIT + 2):
                          if g5 >= 2:
                              j = g5 - 2
                              b = j % 2
                              v.wait_ge(s_od, 16 * j)
                              v.wait_ge(s_lx2, 16 * (j + 1))
                              v.tensor_tensor(ff[b][:, :], ff[b][:, :], bc(M5b[b]), ALU.subtract)
                              v.tensor_tensor(ff[b][:, :], ff[b][:, :], bc(S5b[b]), ALU.mult)
                              v.tensor_tensor(outt[b][:, :], ff[b][:, :], x2_l1[:, :], ALU.add).then_inc(s_out5, 1)
                          if 1 <= g5 <= NIT:
                              j = g5 - 1
                              b = j % 2
                              v.wait_ge(s_se5, j + 1)
                              v.tensor_tensor(tr1[:, :], M5b[b][:, :], M5b[b][:, :], ALU.mult)
                              v.tensor_tensor(ve5[b][:, :], ve5[b][:, :], tr1[:, :], ALU.subtract)
                              v.tensor_scalar(ve5[b][:, :], ve5[b][:, :], 1.0, EPS, ALU.mult, ALU.add)
                              _rsqrt_seed(v, S5b[b][:, :], ve5[b][:, :])
                              _rsqrt_newton(v, S5b[b][:, :], ve5[b][:, :], gr1[:, :])

        if phases != (1, 2, 3, 4, 5, 6):
            with nc.Block("drain") as blk:
                @blk.sync
                def _(sy):
                    if 1 in phases:
                        sy.wait_ge(sp_vpre, 16 * NIT)
                        sy.wait_ge(sp_sg, 16 * NIT)
                        sy.wait_ge(sp_y, 16 * NIT)
                    if 2 in phases and PH2LVL >= 8:
                        sy.wait_ge(sp_a, 16 * NIT)
                        sy.wait_ge(sp_b, 16 * NIT)
                        sy.wait_ge(sp_fwd, 16 * NIT)
                    if 3 in phases:
                        sy.wait_ge(sp_at, 16 * NIT)
                    if 4 in phases:
                        sy.wait_ge(sp_x2, 16 * NIT)
                        sy.wait_ge(sp_u, 16 * NIT)
                    if 5 in phases:
                        sy.wait_ge(sp_h, 16 * NIT)

    return nc


_NC_CACHE = {}


def _get_nc():
    if "nc" not in _NC_CACHE:
        _NC_CACHE["nc"] = build_program()
    return _NC_CACHE["nc"]


def kernel(L0_attn_x, L0_attn_ln_gamma, L0_attn_ln_beta, L0_attn_W_in, L0_attn_b_in,
           L0_attn_conv_w, L0_attn_conv_b, L0_attn_W_delta, L0_attn_b_delta,
           L0_attn_W_out, L0_attn_b_out, L0_ffn_w_gate, L0_ffn_w_up, L0_ffn_w_down):
    x = np.asarray(L0_attn_x, dtype=np.float32)
    w_in = np.ascontiguousarray(np.asarray(L0_attn_W_in, dtype=np.float32))
    w_delta = np.ascontiguousarray(np.asarray(L0_attn_W_delta, dtype=np.float32))
    w_out = np.ascontiguousarray(np.asarray(L0_attn_W_out, dtype=np.float32))
    wgT = np.ascontiguousarray(np.asarray(L0_ffn_w_gate, dtype=np.int32).T)
    wuT = np.ascontiguousarray(np.asarray(L0_ffn_w_up, dtype=np.int32).T)
    wdT = np.ascontiguousarray(np.asarray(L0_ffn_w_down, dtype=np.int32).T)
    conv_w = np.asarray(L0_attn_conv_w, dtype=np.float32)
    convd = np.zeros((3, KI, 128, 128), dtype=np.float32)
    idx = np.arange(128)
    for k3 in range(3):
        for cc in range(KI):
            convd[k3, cc, idx, idx] = conv_w[cc * 128 + idx, k3]
    gam = np.ascontiguousarray(np.asarray(L0_attn_ln_gamma, np.float32).reshape(KD, 128).T)
    bet = np.ascontiguousarray(np.asarray(L0_attn_ln_beta, np.float32).reshape(KD, 128).T)
    b_in = np.ascontiguousarray(np.asarray(L0_attn_b_in, np.float32).reshape(32, 128).T)
    conv_b = np.ascontiguousarray(np.asarray(L0_attn_conv_b, np.float32).reshape(KI, 128).T)
    b_delta = np.ascontiguousarray(np.asarray(L0_attn_b_delta, np.float32).reshape(KI, 128).T)
    b_out = np.ascontiguousarray(np.asarray(L0_attn_b_out, np.float32))

    shared = dict(
        w_in=w_in.ravel(), w_delta=w_delta.ravel(), w_out=w_out.ravel(),
        wgT=wgT.ravel(), wuT=wuT.ravel(), wdT=wdT.ravel(), convd=convd.ravel(),
        gam=gam.ravel(), bet=bet.ravel(), b_in=b_in.ravel(), conv_b=conv_b.ravel(),
        b_delta=b_delta.ravel(), b_out=b_out.ravel(),
    )
    in_maps = []
    for core in range(8):
        xc = np.ascontiguousarray(
            x[core * BPC:(core + 1) * BPC].transpose(0, 2, 1))
        m = dict(shared)
        m["x"] = xc.ravel()
        in_maps.append(m)

    nc = _get_nc()
    res = run_bass_kernel_spmd(nc, in_maps, core_ids=list(range(8)))
    outs = []
    for core in range(8):
        oc = res.results[core]["out"].reshape(BPC, D, SEQ)
        outs.append(oc.transpose(0, 2, 1))
    return np.ascontiguousarray(np.concatenate(outs, axis=0))
